# version 14
# baseline (speedup 1.0000x reference)
import numpy as np
from contextlib import ExitStack

import concourse.bass as bass
import concourse.mybir as mybir
from concourse.bass_utils import run_bass_kernel_spmd

F32 = mybir.dt.float32
BF16 = mybir.dt.bfloat16
AF = mybir.ActivationFunctionType
ALU = mybir.AluOpType

D = 1024
T = 2048
NH = 16
HD = 64
WIN = 256
NG = T // WIN
NE = 8
HF = 2048
NDT = D // 128
NTT = T // 128
NTC = T // 512
NHT = HF // 128

EXP_SHIFT = -10.0
KB = 1024


class Ctr:
    def __init__(self, stack, nc, name):
        self.sem = stack.enter_context(nc.semaphore(name))
        self.n = 0


class Rec:
    ENGINES = ("tensor", "vector", "scalar", "sync", "gpsimd")

    def __init__(self):
        self.ops = {e: [] for e in self.ENGINES}
        self.last_wait = {e: {} for e in self.ENGINES}

    def emit(self, eng, fn, kind="op"):
        self.ops[eng].append((kind, fn))

    def wait(self, eng, ctr, val):
        if val <= 0:
            return
        key = id(ctr)
        if self.last_wait[eng].get(key, -1) >= val:
            return
        self.last_wait[eng][key] = val
        sem = ctr.sem
        self.emit(eng, lambda E, s=sem, v=val: E.wait_ge(s, v), kind="wait")


def build_program(debug_upto=None):
    nc = bass.Bass("TRN2", target_bir_lowering=False)

    xT_in = nc.dram_tensor("xT", [D, T], F32, kind="ExternalInput")
    wqkvT_in = nc.dram_tensor("wqkvT", [D, 3 * D], F32, kind="ExternalInput")
    wprojT_in = nc.dram_tensor("wprojT", [D, D], F32, kind="ExternalInput")
    gateWT_in = nc.dram_tensor("gateWT", [D, NE], F32, kind="ExternalInput")
    w0T_in = nc.dram_tensor("w0T", [NE, D, HF], F32, kind="ExternalInput")
    w1T_in = nc.dram_tensor("w1T", [NE, D, HF], F32, kind="ExternalInput")
    w2T_in = nc.dram_tensor("w2T", [NE, HF, D], F32, kind="ExternalInput")
    bqkvqk_in = nc.dram_tensor("bqkv_qk", [128, 16], F32, kind="ExternalInput")
    bqkvv_in = nc.dram_tensor("bqkv_v", [D], F32, kind="ExternalInput")
    bproj_in = nc.dram_tensor("bproj", [128, 8], F32, kind="ExternalInput")
    b0c_in = nc.dram_tensor("b0c", [128, 128], F32, kind="ExternalInput")
    b1c_in = nc.dram_tensor("b1c", [128, 128], F32, kind="ExternalInput")
    b2_in = nc.dram_tensor("b2", [NE, D], F32, kind="ExternalInput")
    gateb_in = nc.dram_tensor("gateb", [NE], F32, kind="ExternalInput")
    ident_in = nc.dram_tensor("ident", [128, 128], F32, kind="ExternalInput")
    out_hbm = nc.dram_tensor("outT", [D, T], F32, kind="ExternalOutput")
    gw_hbm = nc.dram_tensor("gw_scratch", [NE, T], F32)

    stack = ExitStack()

    def sb(name, shape, dtype, off):
        return nc.alloc_sbuf_tensor_at(name, shape, dtype, offset=base + off)

    USABLE = 206 * KB
    arena = nc.alloc_sbuf_tensor("arena", [128, USABLE], mybir.dt.uint8)
    base = nc.lookup_mloc(arena).addr

    o = 0
    xn = [sb(f"xn{i}", [128, T], BF16, o + i * 4 * KB) for i in range(NDT)]
    o += NDT * 4 * KB
    gwT_sb = sb("gwT", [NE, T], F32, o); o += 8 * KB
    gwTb_sb = sb("gwTb", [NE, T], BF16, o); o += 4 * KB
    bqkv_cols = sb("bqkvc", [128, 16], F32, o); o += 64
    bproj_cols = sb("bprojc", [128, 8], F32, o); o += 32
    b0cols = sb("b0cols", [128, 128], F32, o); o += 512
    b1cols = sb("b1cols", [128, 128], F32, o); o += 512
    b2sb = sb("b2sb", [NE, D], BF16, o); o += 2 * KB
    bvb = sb("bvb", [128, D], F32, o); o += 4 * KB
    gateb_sb = sb("gatebs", [128, NE], F32, o); o += 32
    gateWT32 = sb("gateWT32", [128, NE * NDT], F32, o); o += 256
    id_f32 = sb("idf32", [128, 128], F32, o); o += 512
    id_bf16 = sb("idbf16", [128, 128], BF16, o); o += 256
    ones128 = sb("ones128", [128, 128], BF16, o); o += 256
    l_sb = sb("lsb", [128, 128], F32, o); o += 512
    gw_sb = sb("gwsb", [128, 128], F32, o); o += 512
    m8_sb = sb("m8", [128, NE], F32, o); o += 32
    negm1 = sb("negm1", [128, 1], F32, o); o += 32
    e1_sb = sb("e1", [128, 1], F32, o); o += 32
    den_sb = sb("den", [128, 1], F32, o); o += 32
    rden_sb = sb("rden", [128, 1], F32, o); o += 32
    p8_sb = sb("p8", [128, NE], F32, o); o += 32
    mask_sb = sb("mask8", [128, NE], F32, o); o += 32
    rc_sb = sb("rc", [128, 4], F32, o); o += 32
    c_eps = sb("c_eps", [128, 1], F32, o); o += 32
    c_shift = sb("c_shift", [128, 1], F32, o); o += 32
    o = (o + 31) & ~31
    OB = (o + 31) & ~31

    xs = [sb(f"xs{i}", [128, T], F32, OB + i * 8 * KB) for i in range(2)]
    xb_sb = sb("xb", [128, T], BF16, OB + 16 * KB)
    xsq_sb = sb("xsq", [128, T], BF16, OB + 20 * KB)
    m_b = sb("m_b", [128, T], F32, OB + 24 * KB)
    r_b = sb("r_b", [128, T], F32, OB + 32 * KB)
    xtmp = sb("xtmp", [128, T], F32, OB + 40 * KB)
    st_a = sb("st_a", [128, 512], F32, OB + 48 * KB)
    st_b = sb("st_b", [128, 512], F32, OB + 50 * KB)
    st_c = sb("st_c", [128, 512], F32, OB + 52 * KB)
    st_d = sb("st_d", [128, 512], F32, OB + 54 * KB)
    xn32 = [sb(f"xn32_{i}", [128, T], F32, OB + 56 * KB + i * 8 * KB) for i in range(2)]
    st_e = sb("st_e", [128, 512], F32, OB + 72 * KB)
    xtmp2 = sb("xtmp2", [128, T], F32, OB + 74 * KB)

    qT = [sb(f"qT{i}", [128, T], BF16, OB + i * 4 * KB) for i in range(NDT)]
    kT = [sb(f"kT{i}", [128, T], BF16, OB + 32 * KB + i * 4 * KB) for i in range(NDT)]
    VO = OB + 64 * KB
    Vaug = [sb(f"V{i}", [128, NH * 65], BF16, VO + i * 2080) for i in range(NTT)]
    QKSL_O = VO + 16 * 2080
    qksl = [sb(f"qksl{i}", [128, NDT * 128], BF16, OB + 56 * KB + i * 2 * KB) for i in range(4)]
    VSL_O = QKSL_O + 4 * KB
    vsl = [sb(f"vsl{i}", [128, NDT * 512], BF16, VSL_O + i * 8 * KB) for i in range(2)]
    EXP_O = VSL_O + 16 * KB
    expb = [sb("exp0", [128, 1024], BF16, EXP_O),
            sb("exp1", [128, 1024], BF16, EXP_O + 2 * KB),
            sb("exp2", [128, 1024], BF16, VSL_O)]
    AO_O = EXP_O + 4 * KB
    AO = [sb(f"AO{i}", [128, D], BF16, AO_O + i * 2 * KB) for i in range(NTT)]
    P2_END = AO_O + 32 * KB

    SUM = [sb(f"SUM{i}", [128, T], BF16, OB + i * 4 * KB) for i in range(NDT)]
    AOT_sb = sb("AOT", [128, NDT * T], BF16, OB + 32 * KB)
    wp_sb = sb("wp", [128, NDT * D], BF16, OB + 64 * KB)

    act = [sb(f"act{i}", [128, T], BF16, OB + 32 * KB + i * 4 * KB) for i in range(NHT)]
    W2T_sb = sb("W2T", [128, NHT * D], BF16, OB + 96 * KB)
    w0sl = [sb(f"w0sl{i}", [128, NDT * 128], BF16, OB + 128 * KB + i * 2 * KB) for i in range(4)]
    w1sl = [sb(f"w1sl{i}", [128, NDT * 128], BF16, OB + 136 * KB + i * 2 * KB) for i in range(4)]
    gwb_sb = sb("gwb", [128, T], BF16, OB + 144 * KB)
    silb = [sb(f"sil{i}", [128, 512], BF16, OB + 148 * KB + i * KB) for i in range(3)]
    tmpb = [sb(f"tmpb{i}", [128, 512], F32, OB + 151 * KB) for i in range(1)]

    xs5 = [sb(f"xs5_{i}", [128, T], F32, OB + 128 * KB + i * 8 * KB) for i in range(2)]
    ob = [sb(f"ob{i}", [128, T], F32, OB + 32 * KB + i * 8 * KB) for i in range(2)]

    pd = [nc.alloc_psum_tensor(f"pd{i}", [128, 1024], F32) for i in range(3)]
    pb67 = [nc.alloc_psum_tensor(f"pb{i}", [128, 512], F32) for i in range(6, 8)]
    pb = [pd[0][:, 0:512], pd[0][:, 512:1024], pd[1][:, 0:512], pd[1][:, 512:1024],
          pd[2][:, 0:512], pd[2][:, 512:1024]] + pb67

    rec = Rec()
    ct = Ctr(stack, nc, "ct")
    cv = Ctr(stack, nc, "cv")
    cs = Ctr(stack, nc, "cs")
    cq = Ctr(stack, nc, "cq")
    cg = Ctr(stack, nc, "cg")
    dcq = Ctr(stack, nc, "dcq")
    dcg = Ctr(stack, nc, "dcg")
    dxs = [Ctr(stack, nc, f"dxs{i}") for i in range(2)]
    dqk = [Ctr(stack, nc, f"dqk{i}") for i in range(4)]
    dvs = [Ctr(stack, nc, f"dvs{i}") for i in range(2)]
    dwp = Ctr(stack, nc, "dwp")
    dgww = Ctr(stack, nc, "dgww")
    dw0 = [Ctr(stack, nc, f"dw0{i}") for i in range(4)]
    dw1 = [Ctr(stack, nc, f"dw1{i}") for i in range(4)]
    dw2 = Ctr(stack, nc, "dw2")
    dgwb = Ctr(stack, nc, "dgwb")
    dstr = [Ctr(stack, nc, f"dst{i}") for i in range(2)]
    ECTR = {"tensor": ct, "vector": cv, "scalar": cs, "sync": cq, "gpsimd": cg}

    def dma(eng, ctr, out_ap, in_ap):
        ctr.n += 16
        v = ctr.n
        sem = ctr.sem
        rec.emit(eng, lambda E, o_=out_ap, i_=in_ap, s=sem: E.dma_start(out=o_, in_=i_).then_inc(s, 16))
        return v

    def step(eng, fn, ctr=None, dr=False):
        kind = "op_dr" if dr else "op"
        if ctr is None:
            rec.emit(eng, fn, kind=kind)
            return None
        ctr.n += 1
        v = ctr.n
        sem = ctr.sem
        rec.emit(eng, lambda E, f=fn, s=sem: f(E).then_inc(s, 1), kind=kind)
        return v

    BCTR = {e: Ctr(stack, nc, f"bar_{e}") for e in Rec.ENGINES}

    def barrier():
        for e in Rec.ENGINES:
            c = BCTR[e]
            c.n += 1
            sem = c.sem
            rec.emit(e, lambda E, s=sem: E.drain().then_inc(s, 1))
        snap = {e: BCTR[e].n for e in Rec.ENGINES}
        for e in Rec.ENGINES:
            for f in Rec.ENGINES:
                if f != e:
                    rec.wait(e, BCTR[f], snap[f])

    def finish_debug(tiles):
        barrier()
        for dt in range(len(tiles)):
            w = min(tiles[dt].shape[-1], T)
            p = tiles[dt].shape[0]
            dma("gpsimd", dstr[0], out_hbm[dt * 128:dt * 128 + p, :w], tiles[dt][:p, :w])
        rec.emit("gpsimd", lambda E, v=dstr[0].n: E.wait_ge(dstr[0].sem, v))

    def finalize():
        def run(E, name):
            for kind, f in rec.ops[name]:
                if kind == "op_dr":
                    E.drain()
                f(E)

        with nc.Block() as block:
            @block.tensor
            def _(t):
                run(t, "tensor")

            @block.vector
            def _(v):
                run(v, "vector")

            @block.scalar
            def _(s):
                run(s, "scalar")

            @block.sync
            def _(s):
                run(s, "sync")

            @block.gpsimd
            def _(g):
                run(g, "gpsimd")

        stack.close()
        return nc



    xload = [0, 0]
    xload[0] = dma("sync", dxs[0], xs[0][:], xT_in[0:128, :])
    dma("sync", dcq, bqkv_cols[:], bqkvqk_in[:])
    dma("sync", dcq, bproj_cols[:], bproj_in[:])
    dma("sync", dcq, b0cols[:], b0c_in[:])
    dma("sync", dcq, b1cols[:], b1c_in[:])
    dma("sync", dcq, id_f32[:], ident_in[:])
    dma("gpsimd", dcg, id_bf16[:], ident_in[:])
    dma("sync", dcq, gateWT32[:], gateWT_in[:, :].rearrange("(a p) e -> p a e", p=128))
    dma("gpsimd", dcg, b2sb[:], b2_in[:])
    dma("gpsimd", dcg, bvb[:], bqkvv_in[None, :].partition_broadcast(128))
    dma("gpsimd", dcg, gateb_sb[:], gateb_in[None, :].partition_broadcast(128))
    consts_q = dcq.n
    consts_g = dcg.n
    step("vector", lambda v: v.memset(c_eps[:], 1e-5))
    step("vector", lambda v: v.memset(c_shift[:], EXP_SHIFT))
    v_ones = step("vector", lambda v: v.memset(ones128[:], 1.0), cv)

    if debug_upto == "c0":
        rec.wait("vector", dcq, consts_q)
        rec.wait("vector", dcg, consts_g)
        finish_debug([b0cols, b1cols, bvb, gateb_sb, gateWT32, b2sb, id_f32, bqkv_cols])
        return finalize()

    xfree_v = [0, 0]
    xfree_s = [0, 0]
    sums_done = 0
    for dt in range(NDT):
        slot = dt % 2
        qeng = "sync" if dt % 2 == 0 else "scalar"
        if dt > 0:
            rec.wait(qeng, cv, xfree_v[slot])
            rec.wait(qeng, cs, xfree_s[slot])
            xload[slot] = dma(qeng, dxs[slot], xs[slot][:], xT_in[dt * 128:(dt + 1) * 128, :])
        rec.wait("vector", dxs[slot], xload[slot])
        if dt >= 1:
            rec.wait("vector", ct, sums_done)
        xfree_v[slot] = step("vector", lambda v, s=slot: v.tensor_copy(xb_sb[:], xs[s][:]), cv)
        rec.wait("scalar", dxs[slot], xload[slot])
        if dt >= 1:
            rec.wait("scalar", ct, sums_done)
        xfree_s[slot] = step("scalar", lambda s_, s=slot: s_.activation(
            out=xsq_sb[:], in_=xs[s][:], func=AF.Square), cs)
        rec.wait("tensor", cv, max(xfree_v[slot], v_ones))
        rec.wait("tensor", cs, xfree_s[slot])
        for c in range(NTC):
            st = (dt == 0)
            sp = (dt == NDT - 1)
            step("tensor", lambda t, c=c, st=st, sp=sp: t.matmul(
                pb[c][:], ones128[:], xb_sb[:, c * 512:(c + 1) * 512], start=st, stop=sp))
            last = (c == NTC - 1)
            sums_done = step("tensor", lambda t, c=c, st=st, sp=sp: t.matmul(
                pb[4 + c][:], ones128[:], xsq_sb[:, c * 512:(c + 1) * 512], start=st, stop=sp),
                ct if last else None) or sums_done

    if debug_upto == "p0s":
        rec.wait("vector", ct, sums_done)
        finish_debug([pb[0], pb[1], pb[2], pb[3], pb[4], pb[5], pb[6], pb[7]])
        return finalize()

    def heater(gate_val, bank):
        rec.wait("tensor", cv, gate_val)
        step("tensor", lambda t, b=bank: t.matmul(
            pb[b][:], ones128[:], xb_sb[:, 0:512], start=True, stop=True))

    LVL = {"p0m_m": 1, "p0m_ve": 2, "p0m_sq": 3, "p0m_rc": 4}.get(debug_upto, 99)
    rec.wait("vector", ct, sums_done)
    stats_done = 0
    sq_done = 0
    for c in range(NTC):
        sl = slice(c * 512, (c + 1) * 512)
        rec.wait("vector", cs, sq_done)
        if c >= 1:
            heater(cv.n, c - 1)
        step("vector", lambda v, sl=sl, c=c: v.tensor_scalar(
            out=m_b[:, sl], in0=pb[c][:], scalar1=1.0 / D, scalar2=None, op0=ALU.mult), dr=True)
        if LVL < 2:
            continue
        step("vector", lambda v, sl=sl: v.tensor_tensor(st_a[:], m_b[:, sl], m_b[:, sl], ALU.mult), dr=True)
        ve_done = step("vector", lambda v, c=c: v.scalar_tensor_tensor(
            out=st_e[:], in0=pb[4 + c][:], scalar=1.0 / D, in1=st_a[:],
            op0=ALU.mult, op1=ALU.subtract), None, dr=True)
        ve_done = step("vector", lambda v: v.tensor_scalar(
            out=st_b[:], in0=st_e[:], scalar1=1e-5, scalar2=None, op0=ALU.add), cv, dr=True)
        if LVL < 3:
            continue
        rec.wait("scalar", cv, ve_done)
        rec.wait("scalar", cv, v_ones)
        sq_done = step("scalar", lambda s: s.activation(
            out=st_c[:], in_=st_b[:], func=AF.Sqrt), cs)
        if LVL < 4:
            continue
        rec.wait("vector", cs, sq_done)
        step("vector", lambda v: v.reciprocal(st_d[:], st_c[:]), dr=True)
        if LVL < 5:
            continue
        step("vector", lambda v: v.tensor_tensor(st_a[:], st_d[:], st_d[:], ALU.mult), dr=True)
        step("vector", lambda v: v.scalar_tensor_tensor(
            out=st_c[:], in0=st_a[:], scalar=-0.5, in1=st_b[:], op0=ALU.mult, op1=ALU.mult), dr=True)
        stats_done = step("vector", lambda v, sl=sl: v.scalar_tensor_tensor(
            out=r_b[:, sl], in0=st_c[:], scalar=1.5, in1=st_d[:], op0=ALU.add, op1=ALU.mult), cv, dr=True)
        if c >= 1:
            heater(cv.n - 3, c - 1)

    if debug_upto == "p0m_m":
        finish_debug([m_b])
        return finalize()
    if debug_upto == "p0m_ve":
        finish_debug([m_b, st_a, st_b])
        return finalize()
    if debug_upto in ("p0m", "p0m_sq", "p0m_rc"):
        finish_debug([m_b, st_a, st_b, st_c, st_d, r_b])
        return finalize()

    x32d = [0, 0]
    lgdone = [0, 0]
    xncopy = [0, 0]
    xfree_ctr = [cv, cv]
    for dt in range(NDT):
        slot = dt % 2
        neng = "vector" if slot == 0 else "gpsimd"
        nctr = cv if slot == 0 else cg
        ntmp = xtmp if slot == 0 else xtmp2
        qeng = "sync" if dt % 2 == 0 else "scalar"
        rec.wait(qeng, xfree_ctr[slot], xfree_v[slot])
        rec.wait(qeng, cs, xfree_s[slot])
        xload[slot] = dma(qeng, dxs[slot], xs[slot][:], xT_in[dt * 128:(dt + 1) * 128, :])
        rec.wait(neng, dxs[slot], xload[slot])
        if slot == 1:
            rec.wait(neng, cv, stats_done)
        xfree_v[slot] = step(neng, lambda v, s=slot, tm=ntmp: v.tensor_tensor(
            tm[:], xs[s][:], m_b[:], ALU.subtract), nctr, dr=True)
        xfree_ctr[slot] = nctr
        if slot == 0:
            heater(xfree_v[slot], 5)
        rec.wait(neng, ct, lgdone[slot])
        rec.wait(neng, cs, xncopy[slot])
        x32d[slot] = step(neng, lambda v, s=slot, tm=ntmp: v.tensor_tensor(
            xn32[s][:], tm[:], r_b[:], ALU.mult), nctr, dr=True)
        rec.wait("scalar", nctr, x32d[slot])
        xncopy[slot] = step("scalar", lambda s_, s=slot, dt=dt: s_.activation(
            out=xn[dt][:], in_=xn32[s][:], func=AF.Copy), cs)
        rec.wait("tensor", nctr, max(x32d[slot], stats_done if slot == 0 else 0))
        rec.wait("tensor", cv, stats_done)
        rec.wait("tensor", dcq, consts_q)
        for tt in range(NTT):
            first = (dt == 0 and tt == 0)
            last_mm = (dt == NDT - 1 and tt == NTT - 1)
            lgdone[slot] = step("tensor", lambda t, s=slot, tt=tt, dt=dt, first=first, last_mm=last_mm: t.matmul(
                pd[0][:, tt * 8:(tt + 1) * 8], xn32[s][:, tt * 128:(tt + 1) * 128],
                gateWT32[:, dt * 8:(dt + 1) * 8],
                start=first, stop=last_mm),
                ct if tt == NTT - 1 else None) or lgdone[slot]

    lg_mm_done = lgdone[(NDT - 1) % 2]

    lshf = st_a[:, 0:128]
    p8f = st_a[:, 128:256]
    maskf = st_a[:, 256:384]
    gwpf = st_a[:, 384:512]
    mxs = st_b[:, 0:16]
    m2s = st_b[:, 16:32]
    d21s = st_b[:, 32:48]
    e1s = st_b[:, 48:64]
    dens = st_b[:, 64:80]
    rdens = st_b[:, 80:96]
    l3 = l_sb[:].rearrange("p (t e) -> p t e", e=8)
    rec.wait("vector", ct, lg_mm_done)
    rec.wait("vector", dcg, consts_g)
    lbias_ap = gateb_sb[:, None, :].broadcast_to([128, NTT, NE])
    lpsum_ap = pd[0][:, 0:128].rearrange("p (t e) -> p t e", e=8)
    step("vector", lambda v: v.tensor_tensor(l3, lpsum_ap, lbias_ap, ALU.add), dr=True)
    step("vector", lambda v: v.tensor_reduce(
        mxs[:].rearrange("p (t o) -> p t o", o=1), l3, mybir.AxisListType.X, ALU.max), dr=True)
    mx3 = mxs[:, :, None].broadcast_to([128, NTT, NE])
    step("vector", lambda v: v.tensor_tensor(
        maskf[:].rearrange("p (t e) -> p t e", e=8), l3, mx3, ALU.is_equal), dr=True)
    step("vector", lambda v: v.scalar_tensor_tensor(
        out=lshf[:].rearrange("p (t e) -> p t e", e=8),
        in0=maskf[:].rearrange("p (t e) -> p t e", e=8), scalar=-1e30, in1=l3,
        op0=ALU.mult, op1=ALU.add), dr=True)
    step("vector", lambda v: v.tensor_reduce(
        m2s[:].rearrange("p (t o) -> p t o", o=1),
        lshf[:].rearrange("p (t e) -> p t e", e=8), mybir.AxisListType.X, ALU.max), dr=True)
    d21_done = step("vector", lambda v: v.tensor_tensor(d21s[:], m2s[:], mxs[:], ALU.subtract), cv, dr=True)
    rec.wait("scalar", cv, d21_done)
    e1_done = step("scalar", lambda s: s.activation(out=e1s[:], in_=d21s[:], func=AF.Exp), cs)
    m23 = m2s[:, :, None].broadcast_to([128, NTT, NE])
    step("vector", lambda v: v.tensor_tensor(
        lshf[:].rearrange("p (t e) -> p t e", e=8), l3, mx3, ALU.subtract), dr=True)
    lsh_done = step("vector", lambda v: v.tensor_tensor(
        maskf[:].rearrange("p (t e) -> p t e", e=8), l3, m23, ALU.is_ge), cv, dr=True)
    rec.wait("scalar", cv, lsh_done)
    p8_done = step("scalar", lambda s: s.activation(out=p8f[:], in_=lshf[:], func=AF.Exp), cs)
    rec.wait("vector", cs, e1_done)
    step("vector", lambda v: v.tensor_scalar(
        out=dens[:], in0=e1s[:], scalar1=1.0, scalar2=None, op0=ALU.add), dr=True)
    step("vector", lambda v: v.reciprocal(rdens[:], dens[:]), dr=True)
    rec.wait("vector", cs, p8_done)
    step("vector", lambda v: v.tensor_tensor(gwpf[:], p8f[:], maskf[:], ALU.mult), dr=True)
    rd3 = rdens[:, :, None].broadcast_to([128, NTT, NE])
    gw_done = step("vector", lambda v: v.tensor_tensor(
        gw_sb[:].rearrange("p (t e) -> p t e", e=8),
        gwpf[:].rearrange("p (t e) -> p t e", e=8), rd3, ALU.mult), cv, dr=True)

    gwt_evac = 0
    for tt in range(NTT):
        bk = 6 + tt % 2
        rec.wait("tensor", cv, gw_done)
        rec.wait("tensor", cv, gwt_evac)
        tr_done = step("tensor", lambda t, bk=bk, tt=tt: t.matmul(
            pb[bk][0:8, 0:128], gw_sb[:, tt * 8:(tt + 1) * 8], id_f32[:],
            start=True, stop=True), ct)
        rec.wait("vector", ct, tr_done)
        gwt_evac = step("vector", lambda v, bk=bk, tt=tt: v.tensor_copy(
            gwT_sb[:, tt * 128:(tt + 1) * 128], pb[bk][0:8, 0:128]), cv)

    rec.wait("sync", cv, gwt_evac)
    gwT_hbm_done = dma("sync", dgww, gw_hbm[:], gwT_sb[:])
    gwTb_done = step("vector", lambda v: v.tensor_copy(gwTb_sb[:], gwT_sb[:]), cv, dr=True)

    if debug_upto == "p0":
        finish_debug(xn)
        return finalize()
    if debug_upto == "p1gw":
        finish_debug([gw_sb, gwT_sb, l_sb])
        return finalize()
    barrier()

    qksl_done = [0, 0, 0, 0]
    qk_mm_done = {}
    qk_evac_done = {}

    def emit_qksl_dma(ot):
        slot = ot % 4
        if ot >= 4:
            rec.wait("gpsimd", ct, qk_mm_done[ot - 4])
        qksl_done[slot] = dma("gpsimd", dqk[slot], qksl[slot][:],
                              wqkvT_in[:, ot * 128:(ot + 1) * 128].rearrange("(a p) o -> p a o", p=128))

    for _o in range(4):
        emit_qksl_dma(_o)
    for ot in range(16):
        slot = ot % 4
        bs = (ot % 2) * 4
        rec.wait("tensor", dqk[slot], qksl_done[slot])
        if ot >= 2:
            rec.wait("tensor", cv, qk_evac_done[ot - 2])
        for dt in range(NDT):
            for c in range(NTC):
                last = (dt == NDT - 1 and c == NTC - 1)
                qk_mm_done[ot] = step("tensor", lambda t, s=slot, bs=bs, dt=dt, c=c: t.matmul(
                    pb[bs + c][:], qksl[s][:, dt * 128:(dt + 1) * 128],
                    xn[dt][:, c * 512:(c + 1) * 512],
                    start=(dt == 0), stop=(dt == NDT - 1)), ct if last else None) or qk_mm_done.get(ot, 0)
        if ot + 4 < 16:
            emit_qksl_dma(ot + 4)
        rec.wait("vector", ct, qk_mm_done[ot])
        rec.wait("vector", dcq, consts_q)
        dest = qT[ot] if ot < 8 else kT[ot - 8]
        for c in range(NTC):
            last = (c == NTC - 1)
            qk_evac_done[ot] = step("vector", lambda v, d=dest, bs=bs, c=c, ot=ot: v.tensor_scalar(
                out=d[:, c * 512:(c + 1) * 512], in0=pb[bs + c][:],
                scalar1=bqkv_cols[:, ot:ot + 1], scalar2=None, op0=ALU.add),
                cv if last else None) or qk_evac_done.get(ot, 0)

    vsl_done = [0, 0]
    for c in range(2):
        vsl_done[c] = dma("gpsimd", dvs[c], vsl[c][:],
                          wqkvT_in[:, 2 * D + c * 512: 2 * D + (c + 1) * 512]
                          .rearrange("(a p) o -> p a o", p=128))
    v_mm_done = {}
    v_evac_done = {}
    for tt in range(NTT):
        bs = (tt % 2) * 2
        rec.wait("tensor", dvs[0], vsl_done[0])
        rec.wait("tensor", dvs[1], vsl_done[1])
        rec.wait("tensor", cv, qk_evac_done[15])
        if tt >= 2:
            rec.wait("tensor", cv, v_evac_done[tt - 2])
        for dt in range(NDT):
            for c in range(2):
                last = (dt == NDT - 1 and c == 1)
                v_mm_done[tt] = step("tensor", lambda t, bs=bs, dt=dt, c=c, tt=tt: t.matmul(
                    pb[bs + c][:], xn[dt][:, tt * 128:(tt + 1) * 128],
                    vsl[c][:, dt * 512:(dt + 1) * 512],
                    start=(dt == 0), stop=(dt == NDT - 1)), ct if last else None) or v_mm_done.get(tt, 0)
        rec.wait("vector", ct, v_mm_done[tt])
        rec.wait("vector", dcg, consts_g)
        step("vector", lambda v, tt=tt: v.memset(Vaug[tt][:, 64:NH * 65:65], 1.0))
        for c in range(2):
            last = (c == 1)
            out_ap = Vaug[tt][:, c * 520:(c + 1) * 520].rearrange("p (h x) -> p h x", x=65)[:, :, 0:64]
            in1_ap = bvb[:, c * 512:(c + 1) * 512].rearrange("p (h x) -> p h x", x=64)
            psum_ap = pb[bs + c][:].rearrange("p (h x) -> p h x", x=64)
            v_evac_done[tt] = step("vector", lambda v, o_=out_ap, p_=psum_ap, b_=in1_ap: v.tensor_tensor(
                o_, p_, b_, ALU.add), cv if last else None) or v_evac_done.get(tt, 0)

    if debug_upto == "p1q":
        finish_debug(qT)
        return finalize()
    if debug_upto == "p1k":
        finish_debug(kT)
        return finalize()
    if debug_upto == "p1v":
        finish_debug(Vaug[:8])
        return finalize()
    if debug_upto == "p1v2":
        finish_debug(Vaug[8:])
        return finalize()
    barrier()

    iters = []
    for g in range(NG):
        klist = [kt for kt in range(2 * g - 1, 2 * g + 3) if 0 <= kt < NTT]
        for h in range(NH):
            iters.append((g, h, klist))
    NIT = len(iters)

    dots_free = {}
    av_free = {}
    exp_free = {}
    dots_done_l = {}
    exp_done_l = {}

    def emit_dots(it):
        g, h, klist = iters[it]
        par = it % 3
        pt = h // 2
        po = (h % 2) * 64
        nk = len(klist)
        if it >= 3:
            rec.wait("tensor", cs, dots_free[par])
        dd = 0
        for j, kt in enumerate(klist):
            dst = pd[par][:, j * 256:(j + 1) * 256]
            dd = step("tensor", lambda t, d=dst, pt=pt, po=po, kt=kt, g=g: t.matmul(
                d, kT[pt][po:po + 64, kt * 128:(kt + 1) * 128],
                qT[pt][po:po + 64, g * 256:(g + 1) * 256],
                start=True, stop=True), ct if j == nk - 1 else None) or dd
        dots_done_l[it] = dd

    def emit_exp(it):
        g, h, klist = iters[it]
        par = it % 3
        nk = len(klist)
        rec.wait("scalar", ct, dots_done_l[it])
        if it >= 3:
            rec.wait("scalar", ct, exp_free[par])
        exp_done_l[it] = step("scalar", lambda s, e=par, par=par, nk=nk: s.activation(
            out=expb[e][:, 0:nk * 256], in_=pd[par][:, 0:nk * 256],
            func=AF.Exp, bias=c_shift[:], scale=1.0 / 8.0), cs)
        dots_free[par] = exp_done_l[it]

    def emit_av(it):
        g, h, klist = iters[it]
        par = it % 3
        abk = 6 + it % 2
        nk = len(klist)
        rec.wait("tensor", cs, exp_done_l[it])
        if it >= 2:
            rec.wait("tensor", cv, av_free[it % 2])
        av_done = 0
        for qi in range(2):
            for j, kt in enumerate(klist):
                last = (qi == 1 and j == nk - 1)
                av_done = step("tensor", lambda t, e=par, j=j, qi=qi, kt=kt, abk=abk, h=h, nk=nk: t.matmul(
                    pb[abk][:, qi * 65:qi * 65 + 65],
                    expb[e][:, j * 256 + qi * 128:j * 256 + qi * 128 + 128],
                    Vaug[kt][:, h * 65:(h + 1) * 65],
                    start=(j == 0), stop=(j == nk - 1)), ct if last else None) or av_done
        exp_free[par] = av_done
        rec.wait("vector", ct, av_done)
        ev = 0
        for qi in range(2):
            rcc = (it % 2) * 2 + qi
            step("vector", lambda v, abk=abk, qi=qi, rcc=rcc: v.reciprocal(
                rc_sb[:, rcc:rcc + 1], pb[abk][:, qi * 65 + 64:qi * 65 + 65]))
            ev = step("vector", lambda v, abk=abk, qi=qi, g=g, h=h, rcc=rcc: v.tensor_scalar(
                out=AO[g * 2 + qi][:, h * 64:(h + 1) * 64],
                in0=pb[abk][:, qi * 65:qi * 65 + 64],
                scalar1=rc_sb[:, rcc:rcc + 1], scalar2=None, op0=ALU.mult),
                cv if qi == 1 else None, dr=True) or ev
        av_free[it % 2] = ev

    emit_dots(0)
    for it in range(NIT):
        emit_exp(it)
        if it + 1 < NIT:
            emit_dots(it + 1)
        emit_av(it)

    if debug_upto == "p2a":
        finish_debug(AO[:8])
        return finalize()
    if debug_upto == "p2b":
        finish_debug(AO[8:])
        return finalize()
    barrier()

    wp_done = dma("gpsimd", dwp, wp_sb[:],
                  wprojT_in[:, :].rearrange("(a p) o -> p a o", p=128))

    tr_evac = {}
    for tt in range(NTT):
        bk = tt % 2
        if tt >= 2:
            rec.wait("tensor", cv, tr_evac[tt - 2])
        tdone = 0
        for c8 in range(8):
            last = (c8 == 7)
            tdone = step("tensor", lambda t, tt=tt, c8=c8, bk=bk: t.matmul(
                pd[bk][:, c8 * 128:(c8 + 1) * 128],
                AO[tt][:, c8 * 128:(c8 + 1) * 128], id_bf16[:],
                start=True, stop=True),
                ct if last else None) or tdone
        rec.wait("vector", ct, tdone)
        out_ap = AOT_sb[:, :].rearrange("p (c x) -> p c x", x=T)[:, :, tt * 128:(tt + 1) * 128]
        in_ap = pd[bk][:].rearrange("p (c x) -> p c x", x=128)
        tr_evac[tt] = step("vector", lambda v, o_=out_ap, i_=in_ap: v.tensor_copy(o_, i_), cv)

    proj_mm = {}
    proj_evac = {}
    for ot in range(NDT):
        bs = (ot % 2) * 4
        rec.wait("tensor", dwp, wp_done)
        rec.wait("tensor", cv, tr_evac[NTT - 1])
        if ot >= 2:
            rec.wait("tensor", cv, proj_evac[ot - 2])
        for dt in range(NDT):
            for c in range(NTC):
                last = (dt == NDT - 1 and c == NTC - 1)
                proj_mm[ot] = step("tensor", lambda t, ot=ot, dt=dt, c=c, bs=bs: t.matmul(
                    pb[bs + c][:], wp_sb[:, dt * D + ot * 128:dt * D + ot * 128 + 128],
                    AOT_sb[:, dt * T + c * 512:dt * T + (c + 1) * 512],
                    start=(dt == 0), stop=(dt == NDT - 1)), ct if last else None) or proj_mm.get(ot, 0)
        rec.wait("vector", ct, proj_mm[ot])
        for c in range(NTC):
            last = (c == NTC - 1)
            proj_evac[ot] = step("vector", lambda v, ot=ot, c=c, bs=bs: v.tensor_scalar(
                out=SUM[ot][:, c * 512:(c + 1) * 512], in0=pb[bs + c][:],
                scalar1=bproj_cols[:, ot:ot + 1], scalar2=None, op0=ALU.add),
                cv if last else None) or proj_evac.get(ot, 0)

    b2_evac = 0
    for dt in range(NDT):
        for c in range(NTC):
            bk = 6 + (dt * NTC + c) % 2
            rec.wait("tensor", cv, max(gwTb_done, b2_evac, proj_evac[NDT - 1]))
            rec.wait("tensor", dcg, consts_g)
            bdone = step("tensor", lambda t, dt=dt, c=c, bk=bk: t.matmul(
                pb[bk][:], b2sb[:, dt * 128:(dt + 1) * 128],
                gwTb_sb[:, c * 512:(c + 1) * 512], start=True, stop=True), ct)
            rec.wait("vector", ct, bdone)
            rec.wait("vector", cv, proj_evac.get(dt, 0))
            b2_evac = step("vector", lambda v, dt=dt, c=c, bk=bk: v.tensor_tensor(
                SUM[dt][:, c * 512:(c + 1) * 512], SUM[dt][:, c * 512:(c + 1) * 512],
                pb[bk][:], ALU.add), cv)

    if debug_upto == "p3":
        finish_debug(SUM)
        return finalize()

    w2_loaded = 0
    gwb_loaded = 0
    sl_loaded = {}
    h_mm_done = {}
    act_evac = {}
    w2_evac_d = {}
    b_last_mm = 0

    def emit_slice_dma(e, ht):
        slot = ht % 4
        if (e, ht - 4) in h_mm_done:
            rec.wait("gpsimd", ct, h_mm_done[(e, ht - 4)])
        elif e > 0 and ht < 4:
            rec.wait("gpsimd", ct, h_mm_done[(e - 1, NHT - 4 + ht)])
        a = dma("gpsimd", dw0[slot], w0sl[slot][:],
                w0T_in[e][:, ht * 128:(ht + 1) * 128].rearrange("(a p) o -> p a o", p=128))
        b = dma("gpsimd", dw1[slot], w1sl[slot][:],
                w1T_in[e][:, ht * 128:(ht + 1) * 128].rearrange("(a p) o -> p a o", p=128))
        sl_loaded[(e, ht)] = (slot, a, b)


    rec.wait("gpsimd", ct, proj_mm[NDT - 1])
    for _h in range(4):
        emit_slice_dma(0, _h)
    rec.wait("gpsimd", dgww, gwT_hbm_done)
    w2_loaded = dma("gpsimd", dw2, W2T_sb[:],
                    w2T_in[0][:, :].rearrange("(a p) d -> p a d", p=128))
    gwb_loaded = dma("gpsimd", dgwb, gwb_sb[:], gw_hbm[0][None, :].partition_broadcast(128))
    barrier()


    for e in range(NE):
        for ht in range(NHT):
            slot = ht % 4
            bcol = e * 16 + ht
            _sl, _a, _b = sl_loaded[(e, ht)]
            rec.wait("tensor", dw0[_sl], _a)
            rec.wait("tensor", dw1[_sl], _b)
            for tc in range(NTC):
                i = ht * NTC + tc
                par = i % 3
                if i >= 3:
                    rec.wait("tensor", cv, act_evac[(e, i - 3)])
                for dt in range(NDT):
                    step("tensor", lambda t, s=slot, dt=dt, tc=tc, par=par: t.matmul(
                        pb[par * 2][:], w0sl[s][:, dt * 128:(dt + 1) * 128],
                        xn[dt][:, tc * 512:(tc + 1) * 512],
                        start=(dt == 0), stop=(dt == NDT - 1)))
                for dt in range(NDT):
                    last = (dt == NDT - 1)
                    h_mm_done[(e, ht)] = step("tensor", lambda t, s=slot, dt=dt, tc=tc, par=par: t.matmul(
                        pb[par * 2 + 1][:], w1sl[s][:, dt * 128:(dt + 1) * 128],
                        xn[dt][:, tc * 512:(tc + 1) * 512],
                        start=(dt == 0), stop=(dt == NDT - 1)), ct if last else None) \
                        or h_mm_done.get((e, ht), 0)
                hm = h_mm_done[(e, ht)]
                rec.wait("scalar", ct, hm)
                if i >= 3:
                    rec.wait("scalar", cv, act_evac[(e, i - 3)])
                sil_done = step("scalar", lambda s, par=par, bcol=bcol: s.activation(
                    out=silb[par][:], in_=pb[par * 2 + 1][:], func=AF.Silu,
                    bias=b1cols[:, bcol:bcol + 1], scale=1.0), cs)
                rec.wait("vector", cs, sil_done)
                if e > 0:
                    rec.wait("vector", ct, b_last_mm)
                act_evac[(e, i)] = step("vector", lambda v, ht=ht, tc=tc, par=par, bcol=bcol: v.scalar_tensor_tensor(
                    out=act[ht][:, tc * 512:(tc + 1) * 512], in0=pb[par * 2][:],
                    scalar=b0cols[:, bcol:bcol + 1], in1=silb[par][:],
                    op0=ALU.add, op1=ALU.mult), cv)
            if ht + 4 < NHT:
                emit_slice_dma(e, ht + 4)
            elif e + 1 < NE:
                emit_slice_dma(e + 1, ht + 4 - NHT)

        rec.wait("tensor", dw2, w2_loaded)
        rec.wait("vector", dgwb, gwb_loaded)
        last_act = act_evac[(e, NHT * NTC - 1)]
        rec.wait("tensor", cv, last_act)
        for tc in range(NTC):
            for dt in range(NDT):
                j = tc * NDT + dt
                bk = 6 + j % 2
                if j >= 2:
                    rec.wait("tensor", cv, w2_evac_d[(e, j - 2)])
                for ht in range(NHT):
                    last = (ht == NHT - 1)
                    b_last_mm = step("tensor", lambda t, ht=ht, dt=dt, tc=tc, bk=bk: t.matmul(
                        pb[bk][:], W2T_sb[:, ht * D + dt * 128:ht * D + dt * 128 + 128],
                        act[ht][:, tc * 512:(tc + 1) * 512],
                        start=(ht == 0), stop=(ht == NHT - 1)), ct if last else None) or b_last_mm
                rec.wait("vector", ct, b_last_mm)
                step("vector", lambda v, tc=tc, bk=bk, j=j: v.tensor_tensor(
                    tmpb[0][:], pb[bk][:], gwb_sb[:, tc * 512:(tc + 1) * 512], ALU.mult), dr=True)
                w2_evac_d[(e, j)] = step("vector", lambda v, dt=dt, tc=tc, j=j: v.tensor_tensor(
                    SUM[dt][:, tc * 512:(tc + 1) * 512], SUM[dt][:, tc * 512:(tc + 1) * 512],
                    tmpb[0][:], ALU.add), cv, dr=True)
        if e + 1 < NE:
            rec.wait("gpsimd", ct, b_last_mm)
            w2_loaded = dma("gpsimd", dw2, W2T_sb[:],
                            w2T_in[e + 1][:, :].rearrange("(a p) d -> p a d", p=128))
            rec.wait("gpsimd", cv, w2_evac_d[(e, NTC * NDT - 1)])
            gwb_loaded = dma("gpsimd", dgwb, gwb_sb[:],
                             gw_hbm[e + 1][None, :].partition_broadcast(128))

    if debug_upto == "p4":
        finish_debug(SUM)
        return finalize()

    xload = [0, 0]
    add_done = [0, 0]
    store_done = [0, 0]
    a7_done = h_mm_done[(NE - 1, NHT - 1)]
    for dt in range(NDT):
        slot = dt % 2
        qeng = "sync" if dt % 2 == 0 else "scalar"
        rec.wait(qeng, cv if dt % 2 == 0 else cg, add_done[slot])
        rec.wait(qeng, ct, a7_done)
        xload[slot] = dma(qeng, dxs[slot], xs5[slot][:], xT_in[dt * 128:(dt + 1) * 128, :])
        aeng = "vector" if dt % 2 == 0 else "gpsimd"
        actr = cv if dt % 2 == 0 else cg
        rec.wait(aeng, dxs[slot], xload[slot])
        if dt >= 2:
            rec.wait(aeng, dstr[slot], store_done[slot])
        rec.wait(aeng, ct, b_last_mm)
        rec.wait(aeng, cv, w2_evac_d[(NE - 1, 3 * NDT + dt)])
        add_done[slot] = step(aeng, lambda v, s=slot, dt=dt: v.tensor_tensor(
            ob[s][:], SUM[dt][:], xs5[s][:], ALU.add), actr)
        seng = "sync" if dt % 2 == 0 else "scalar"
        rec.wait(seng, actr, add_done[slot])
        store_done[slot] = dma(seng, dstr[slot], out_hbm[dt * 128:(dt + 1) * 128, :], ob[slot][:])

    rec.emit("sync", lambda E, v=dstr[0].n: E.wait_ge(dstr[0].sem, v))
    rec.emit("sync", lambda E, v=dstr[1].n: E.wait_ge(dstr[1].sem, v))


    return finalize()


def kernel(L0_attn_x, L0_attn_w_qkv, L0_attn_b_qkv, L0_attn_w_proj, L0_attn_b_proj,
           L0_ffn_gate_w, L0_ffn_gate_b, L0_ffn_w0, L0_ffn_b0, L0_ffn_w1, L0_ffn_b1,
           L0_ffn_w2, L0_ffn_b2):
    x = np.asarray(L0_attn_x, dtype=np.float32)

    shared = {
        "wqkvT": np.ascontiguousarray(np.asarray(L0_attn_w_qkv, np.float32).T),
        "wprojT": np.ascontiguousarray(np.asarray(L0_attn_w_proj, np.float32).T),
        "gateWT": np.ascontiguousarray(np.asarray(L0_ffn_gate_w, np.float32).T),
        "w0T": np.ascontiguousarray(np.asarray(L0_ffn_w0, np.float32).transpose(0, 2, 1)),
        "w1T": np.ascontiguousarray(np.asarray(L0_ffn_w1, np.float32).transpose(0, 2, 1)),
        "w2T": np.ascontiguousarray(np.asarray(L0_ffn_w2, np.float32).transpose(0, 2, 1)),
        "bqkv_qk": np.ascontiguousarray(np.asarray(L0_attn_b_qkv, np.float32)[:2 * D].reshape(16, 128).T),
        "bqkv_v": np.ascontiguousarray(np.asarray(L0_attn_b_qkv, np.float32)[2 * D:]),
        "bproj": np.ascontiguousarray(np.asarray(L0_attn_b_proj, np.float32).reshape(8, 128).T),
        "b0c": np.ascontiguousarray(np.asarray(L0_ffn_b0, np.float32).reshape(8, 16, 128).transpose(2, 0, 1).reshape(128, 128)),
        "b1c": np.ascontiguousarray(np.asarray(L0_ffn_b1, np.float32).reshape(8, 16, 128).transpose(2, 0, 1).reshape(128, 128)),
        "b2": np.ascontiguousarray(np.asarray(L0_ffn_b2, np.float32)),
        "gateb": np.ascontiguousarray(np.asarray(L0_ffn_gate_b, np.float32)),
        "ident": np.eye(128, dtype=np.float32),
    }

    in_maps = []
    for c in range(8):
        m = dict(shared)
        m["xT"] = np.ascontiguousarray(x[c].T)
        in_maps.append(m)

    nc = build_program()
    res = run_bass_kernel_spmd(nc, in_maps, core_ids=list(range(8)))
    out = np.stack([np.asarray(res.results[c]["outT"]).T for c in range(8)], axis=0)
    return np.ascontiguousarray(out.astype(np.float32))


# version 15
# speedup vs baseline: 1.0187x; 1.0187x over previous
import numpy as np
from contextlib import ExitStack

import concourse.bass as bass
import concourse.mybir as mybir
from concourse.bass_utils import run_bass_kernel_spmd

F32 = mybir.dt.float32
BF16 = mybir.dt.bfloat16
AF = mybir.ActivationFunctionType
ALU = mybir.AluOpType

D = 1024
T = 2048
NH = 16
HD = 64
WIN = 256
NG = T // WIN
NE = 8
HF = 2048
NDT = D // 128
NTT = T // 128
NTC = T // 512
NHT = HF // 128

EXP_SHIFT = -10.0
KB = 1024


class Ctr:
    def __init__(self, stack, nc, name):
        self.sem = stack.enter_context(nc.semaphore(name))
        self.n = 0


class Rec:
    ENGINES = ("tensor", "vector", "scalar", "sync", "gpsimd")

    def __init__(self):
        self.ops = {e: [] for e in self.ENGINES}
        self.last_wait = {e: {} for e in self.ENGINES}

    def emit(self, eng, fn, kind="op"):
        self.ops[eng].append((kind, fn))

    def wait(self, eng, ctr, val):
        if val <= 0:
            return
        key = id(ctr)
        if self.last_wait[eng].get(key, -1) >= val:
            return
        self.last_wait[eng][key] = val
        sem = ctr.sem
        self.emit(eng, lambda E, s=sem, v=val: E.wait_ge(s, v), kind="wait")


def build_program(debug_upto=None):
    nc = bass.Bass("TRN2", target_bir_lowering=False)

    xT_in = nc.dram_tensor("xT", [D, T], F32, kind="ExternalInput")
    wqkvT_in = nc.dram_tensor("wqkvT", [D, 3 * D], F32, kind="ExternalInput")
    wprojT_in = nc.dram_tensor("wprojT", [D, D], F32, kind="ExternalInput")
    gateWT_in = nc.dram_tensor("gateWT", [D, NE], F32, kind="ExternalInput")
    w0T_in = nc.dram_tensor("w0T", [NE, D, HF], F32, kind="ExternalInput")
    w1T_in = nc.dram_tensor("w1T", [NE, D, HF], F32, kind="ExternalInput")
    w2T_in = nc.dram_tensor("w2T", [NE, HF, D], F32, kind="ExternalInput")
    bqkvqk_in = nc.dram_tensor("bqkv_qk", [128, 16], F32, kind="ExternalInput")
    bqkvv_in = nc.dram_tensor("bqkv_v", [D], F32, kind="ExternalInput")
    bproj_in = nc.dram_tensor("bproj", [128, 8], F32, kind="ExternalInput")
    b0c_in = nc.dram_tensor("b0c", [128, 128], F32, kind="ExternalInput")
    b1c_in = nc.dram_tensor("b1c", [128, 128], F32, kind="ExternalInput")
    b2_in = nc.dram_tensor("b2", [NE, D], F32, kind="ExternalInput")
    gateb_in = nc.dram_tensor("gateb", [NE], F32, kind="ExternalInput")
    ident_in = nc.dram_tensor("ident", [128, 128], F32, kind="ExternalInput")
    out_hbm = nc.dram_tensor("outT", [D, T], F32, kind="ExternalOutput")
    gw_hbm = nc.dram_tensor("gw_scratch", [NE, T], F32)

    stack = ExitStack()

    def sb(name, shape, dtype, off):
        return nc.alloc_sbuf_tensor_at(name, shape, dtype, offset=base + off)

    USABLE = 206 * KB
    arena = nc.alloc_sbuf_tensor("arena", [128, USABLE], mybir.dt.uint8)
    base = nc.lookup_mloc(arena).addr

    o = 0
    xn = [sb(f"xn{i}", [128, T], BF16, o + i * 4 * KB) for i in range(NDT)]
    o += NDT * 4 * KB
    gwT_sb = sb("gwT", [NE, T], F32, o); o += 8 * KB
    gwTb_sb = sb("gwTb", [NE, T], BF16, o); o += 4 * KB
    bqkv_cols = sb("bqkvc", [128, 16], F32, o); o += 64
    bproj_cols = sb("bprojc", [128, 8], F32, o); o += 32
    b0cols = sb("b0cols", [128, 128], F32, o); o += 512
    b1cols = sb("b1cols", [128, 128], F32, o); o += 512
    b2sb = sb("b2sb", [NE, D], BF16, o); o += 2 * KB
    bvb = sb("bvb", [128, D], F32, o); o += 4 * KB
    gateb_sb = sb("gatebs", [128, NE], F32, o); o += 32
    gateWT32 = sb("gateWT32", [128, NE * NDT], F32, o); o += 256
    id_f32 = sb("idf32", [128, 128], F32, o); o += 512
    id_bf16 = sb("idbf16", [128, 128], BF16, o); o += 256
    ones128 = sb("ones128", [128, 128], BF16, o); o += 256
    l_sb = sb("lsb", [128, 128], F32, o); o += 512
    gw_sb = sb("gwsb", [128, 128], F32, o); o += 512
    m8_sb = sb("m8", [128, NE], F32, o); o += 32
    negm1 = sb("negm1", [128, 1], F32, o); o += 32
    e1_sb = sb("e1", [128, 1], F32, o); o += 32
    den_sb = sb("den", [128, 1], F32, o); o += 32
    rden_sb = sb("rden", [128, 1], F32, o); o += 32
    p8_sb = sb("p8", [128, NE], F32, o); o += 32
    mask_sb = sb("mask8", [128, NE], F32, o); o += 32
    rc_sb = sb("rc", [128, 4], F32, o); o += 32
    c_eps = sb("c_eps", [128, 1], F32, o); o += 32
    c_shift = sb("c_shift", [128, 1], F32, o); o += 32
    o = (o + 31) & ~31
    OB = (o + 31) & ~31

    xs = [sb(f"xs{i}", [128, T], F32, OB + i * 8 * KB) for i in range(2)]
    xb_sb = sb("xb", [128, T], BF16, OB + 16 * KB)
    xsq_sb = sb("xsq", [128, T], BF16, OB + 20 * KB)
    m_b = sb("m_b", [128, T], F32, OB + 24 * KB)
    r_b = sb("r_b", [128, T], F32, OB + 32 * KB)
    xtmp = sb("xtmp", [128, T], F32, OB + 40 * KB)
    st_a = sb("st_a", [128, 512], F32, OB + 48 * KB)
    st_b = sb("st_b", [128, 512], F32, OB + 50 * KB)
    st_c = sb("st_c", [128, 512], F32, OB + 52 * KB)
    st_d = sb("st_d", [128, 512], F32, OB + 54 * KB)
    xn32 = [sb(f"xn32_{i}", [128, T], F32, OB + 56 * KB + i * 8 * KB) for i in range(2)]
    st_e = sb("st_e", [128, 512], F32, OB + 72 * KB)
    xtmp2 = sb("xtmp2", [128, T], F32, OB + 74 * KB)

    qT = [sb(f"qT{i}", [128, T], BF16, OB + i * 4 * KB) for i in range(NDT)]
    kT = [sb(f"kT{i}", [128, T], BF16, OB + 32 * KB + i * 4 * KB) for i in range(NDT)]
    VO = OB + 64 * KB
    Vaug = [sb(f"V{i}", [128, NH * 65], BF16, VO + i * 2080) for i in range(NTT)]
    QKSL_O = VO + 16 * 2080
    qksl = [sb(f"qksl{i}", [128, NDT * 128], BF16, OB + 56 * KB + i * 2 * KB) for i in range(4)]
    VSL_O = QKSL_O + 4 * KB
    vsl = [sb(f"vsl{i}", [128, NDT * 512], BF16, VSL_O + i * 8 * KB) for i in range(2)]
    EXP_O = VSL_O + 16 * KB
    expb = [sb("exp0", [128, 1024], BF16, EXP_O),
            sb("exp1", [128, 1024], BF16, EXP_O + 2 * KB),
            sb("exp2", [128, 1024], BF16, VSL_O)]
    AO_O = EXP_O + 4 * KB
    AO = [sb(f"AO{i}", [128, D], BF16, AO_O + i * 2 * KB) for i in range(NTT)]
    P2_END = AO_O + 32 * KB

    SUM = [sb(f"SUM{i}", [128, T], BF16, OB + i * 4 * KB) for i in range(NDT)]
    AOT_sb = sb("AOT", [128, NDT * T], BF16, OB + 32 * KB)
    wp_sb = sb("wp", [128, NDT * D], BF16, OB + 64 * KB)

    act = [sb(f"act{i}", [128, T], BF16, OB + 32 * KB + i * 4 * KB) for i in range(NHT)]
    W2T_sb = sb("W2T", [128, NHT * D], BF16, OB + 96 * KB)
    w0sl = [sb(f"w0sl{i}", [128, NDT * 128], BF16, OB + 128 * KB + i * 2 * KB) for i in range(4)]
    w1sl = [sb(f"w1sl{i}", [128, NDT * 128], BF16, OB + 136 * KB + i * 2 * KB) for i in range(4)]
    gwb_sb = sb("gwb", [128, T], BF16, OB + 144 * KB)
    silb = [sb(f"sil{i}", [128, 512], BF16, OB + 148 * KB + i * KB) for i in range(3)]
    tmpb = [sb(f"tmpb{i}", [128, 512], F32, OB + 151 * KB) for i in range(1)]

    xs5 = [sb(f"xs5_{i}", [128, T], F32, OB + 128 * KB + i * 8 * KB) for i in range(2)]
    ob = [sb(f"ob{i}", [128, T], F32, OB + 32 * KB + i * 8 * KB) for i in range(2)]

    pd = [nc.alloc_psum_tensor(f"pd{i}", [128, 1024], F32) for i in range(3)]
    pb67 = [nc.alloc_psum_tensor(f"pb{i}", [128, 512], F32) for i in range(6, 8)]
    pb = [pd[0][:, 0:512], pd[0][:, 512:1024], pd[1][:, 0:512], pd[1][:, 512:1024],
          pd[2][:, 0:512], pd[2][:, 512:1024]] + pb67

    rec = Rec()
    ct = Ctr(stack, nc, "ct")
    cv = Ctr(stack, nc, "cv")
    cs = Ctr(stack, nc, "cs")
    cq = Ctr(stack, nc, "cq")
    cg = Ctr(stack, nc, "cg")
    dcq = Ctr(stack, nc, "dcq")
    dcg = Ctr(stack, nc, "dcg")
    dxs = [Ctr(stack, nc, f"dxs{i}") for i in range(2)]
    dqk = [Ctr(stack, nc, f"dqk{i}") for i in range(4)]
    dvs = [Ctr(stack, nc, f"dvs{i}") for i in range(2)]
    dwp = Ctr(stack, nc, "dwp")
    dgww = Ctr(stack, nc, "dgww")
    dw0 = [Ctr(stack, nc, f"dw0{i}") for i in range(4)]
    dw1 = [Ctr(stack, nc, f"dw1{i}") for i in range(4)]
    dw2 = Ctr(stack, nc, "dw2")
    dgwb = Ctr(stack, nc, "dgwb")
    dstr = [Ctr(stack, nc, f"dst{i}") for i in range(2)]
    ECTR = {"tensor": ct, "vector": cv, "scalar": cs, "sync": cq, "gpsimd": cg}

    def dma(eng, ctr, out_ap, in_ap):
        ctr.n += 16
        v = ctr.n
        sem = ctr.sem
        rec.emit(eng, lambda E, o_=out_ap, i_=in_ap, s=sem: E.dma_start(out=o_, in_=i_).then_inc(s, 16))
        return v

    def step(eng, fn, ctr=None, dr=False):
        kind = "op_dr" if dr else "op"
        if ctr is None:
            rec.emit(eng, fn, kind=kind)
            return None
        ctr.n += 1
        v = ctr.n
        sem = ctr.sem
        rec.emit(eng, lambda E, f=fn, s=sem: f(E).then_inc(s, 1), kind=kind)
        return v

    BCTR = {e: Ctr(stack, nc, f"bar_{e}") for e in Rec.ENGINES}

    def barrier():
        for e in Rec.ENGINES:
            c = BCTR[e]
            c.n += 1
            sem = c.sem
            rec.emit(e, lambda E, s=sem: E.drain().then_inc(s, 1))
        snap = {e: BCTR[e].n for e in Rec.ENGINES}
        for e in Rec.ENGINES:
            for f in Rec.ENGINES:
                if f != e:
                    rec.wait(e, BCTR[f], snap[f])

    def finish_debug(tiles):
        barrier()
        for dt in range(len(tiles)):
            w = min(tiles[dt].shape[-1], T)
            p = tiles[dt].shape[0]
            dma("gpsimd", dstr[0], out_hbm[dt * 128:dt * 128 + p, :w], tiles[dt][:p, :w])
        rec.emit("gpsimd", lambda E, v=dstr[0].n: E.wait_ge(dstr[0].sem, v))

    def finalize():
        def run(E, name):
            for kind, f in rec.ops[name]:
                if kind == "op_dr":
                    E.drain()
                f(E)

        with nc.Block() as block:
            @block.tensor
            def _(t):
                run(t, "tensor")

            @block.vector
            def _(v):
                run(v, "vector")

            @block.scalar
            def _(s):
                run(s, "scalar")

            @block.sync
            def _(s):
                run(s, "sync")

            @block.gpsimd
            def _(g):
                run(g, "gpsimd")

        stack.close()
        return nc



    xload = [0, 0]
    xload[0] = dma("sync", dxs[0], xs[0][:], xT_in[0:128, :])
    dma("sync", dcq, bqkv_cols[:], bqkvqk_in[:])
    dma("sync", dcq, bproj_cols[:], bproj_in[:])
    dma("sync", dcq, b0cols[:], b0c_in[:])
    dma("sync", dcq, b1cols[:], b1c_in[:])
    dma("sync", dcq, id_f32[:], ident_in[:])
    dma("gpsimd", dcg, id_bf16[:], ident_in[:])
    dma("sync", dcq, gateWT32[:], gateWT_in[:, :].rearrange("(a p) e -> p a e", p=128))
    dma("gpsimd", dcg, b2sb[:], b2_in[:])
    dma("gpsimd", dcg, bvb[:], bqkvv_in[None, :].partition_broadcast(128))
    dma("gpsimd", dcg, gateb_sb[:], gateb_in[None, :].partition_broadcast(128))
    consts_q = dcq.n
    consts_g = dcg.n
    step("vector", lambda v: v.memset(c_eps[:], 1e-5))
    step("vector", lambda v: v.memset(c_shift[:], EXP_SHIFT))
    v_ones = step("vector", lambda v: v.memset(ones128[:], 1.0), cv)

    if debug_upto == "c0":
        rec.wait("vector", dcq, consts_q)
        rec.wait("vector", dcg, consts_g)
        finish_debug([b0cols, b1cols, bvb, gateb_sb, gateWT32, b2sb, id_f32, bqkv_cols])
        return finalize()

    xfree_v = [0, 0]
    xfree_s = [0, 0]
    sums_done = 0
    for dt in range(NDT):
        slot = dt % 2
        qeng = "sync" if dt % 2 == 0 else "scalar"
        if dt > 0:
            rec.wait(qeng, cv, xfree_v[slot])
            rec.wait(qeng, cs, xfree_s[slot])
            xload[slot] = dma(qeng, dxs[slot], xs[slot][:], xT_in[dt * 128:(dt + 1) * 128, :])
        rec.wait("vector", dxs[slot], xload[slot])
        if dt >= 1:
            rec.wait("vector", ct, sums_done)
        xfree_v[slot] = step("vector", lambda v, s=slot: v.tensor_copy(xb_sb[:], xs[s][:]), cv)
        rec.wait("scalar", dxs[slot], xload[slot])
        if dt >= 1:
            rec.wait("scalar", ct, sums_done)
        xfree_s[slot] = step("scalar", lambda s_, s=slot: s_.activation(
            out=xsq_sb[:], in_=xs[s][:], func=AF.Square), cs)
        rec.wait("tensor", cv, max(xfree_v[slot], v_ones))
        rec.wait("tensor", cs, xfree_s[slot])
        for c in range(NTC):
            st = (dt == 0)
            sp = (dt == NDT - 1)
            step("tensor", lambda t, c=c, st=st, sp=sp: t.matmul(
                pb[c][:], ones128[:], xb_sb[:, c * 512:(c + 1) * 512], start=st, stop=sp))
            last = (c == NTC - 1)
            sums_done = step("tensor", lambda t, c=c, st=st, sp=sp: t.matmul(
                pb[4 + c][:], ones128[:], xsq_sb[:, c * 512:(c + 1) * 512], start=st, stop=sp),
                ct if last else None) or sums_done

    if debug_upto == "p0s":
        rec.wait("vector", ct, sums_done)
        finish_debug([pb[0], pb[1], pb[2], pb[3], pb[4], pb[5], pb[6], pb[7]])
        return finalize()

    def heater(gate_val, bank):
        rec.wait("tensor", cv, gate_val)
        step("tensor", lambda t, b=bank: t.matmul(
            pb[b][:], ones128[:], xb_sb[:, 0:512], start=True, stop=True))

    LVL = {"p0m_m": 1, "p0m_ve": 2, "p0m_sq": 3, "p0m_rc": 4}.get(debug_upto, 99)
    rec.wait("vector", ct, sums_done)
    stats_done = 0
    sq_done = 0
    for c in range(NTC):
        sl = slice(c * 512, (c + 1) * 512)
        rec.wait("vector", cs, sq_done)
        if c >= 1:
            heater(cv.n, c - 1)
        step("vector", lambda v, sl=sl, c=c: v.tensor_scalar(
            out=m_b[:, sl], in0=pb[c][:], scalar1=1.0 / D, scalar2=None, op0=ALU.mult), dr=True)
        if LVL < 2:
            continue
        step("vector", lambda v, sl=sl: v.tensor_tensor(st_a[:], m_b[:, sl], m_b[:, sl], ALU.mult), dr=True)
        ve_done = step("vector", lambda v, c=c: v.scalar_tensor_tensor(
            out=st_e[:], in0=pb[4 + c][:], scalar=1.0 / D, in1=st_a[:],
            op0=ALU.mult, op1=ALU.subtract), None, dr=True)
        ve_done = step("vector", lambda v: v.tensor_scalar(
            out=st_b[:], in0=st_e[:], scalar1=1e-5, scalar2=None, op0=ALU.add), cv, dr=True)
        if LVL < 3:
            continue
        rec.wait("scalar", cv, ve_done)
        rec.wait("scalar", cv, v_ones)
        sq_done = step("scalar", lambda s: s.activation(
            out=st_c[:], in_=st_b[:], func=AF.Sqrt), cs)
        if LVL < 4:
            continue
        rec.wait("vector", cs, sq_done)
        step("vector", lambda v: v.reciprocal(st_d[:], st_c[:]), dr=True)
        if LVL < 5:
            continue
        step("vector", lambda v: v.tensor_tensor(st_a[:], st_d[:], st_d[:], ALU.mult), dr=True)
        step("vector", lambda v: v.scalar_tensor_tensor(
            out=st_c[:], in0=st_a[:], scalar=-0.5, in1=st_b[:], op0=ALU.mult, op1=ALU.mult), dr=True)
        stats_done = step("vector", lambda v, sl=sl: v.scalar_tensor_tensor(
            out=r_b[:, sl], in0=st_c[:], scalar=1.5, in1=st_d[:], op0=ALU.add, op1=ALU.mult), cv, dr=True)
        if c >= 1:
            heater(cv.n - 3, c - 1)

    if debug_upto == "p0m_m":
        finish_debug([m_b])
        return finalize()
    if debug_upto == "p0m_ve":
        finish_debug([m_b, st_a, st_b])
        return finalize()
    if debug_upto in ("p0m", "p0m_sq", "p0m_rc"):
        finish_debug([m_b, st_a, st_b, st_c, st_d, r_b])
        return finalize()

    x32d = [0, 0]
    lgdone = [0, 0]
    xncopy = [0, 0]
    xfree_ctr = [cv, cv]
    for dt in range(NDT):
        slot = dt % 2
        neng = "vector" if slot == 0 else "gpsimd"
        nctr = cv if slot == 0 else cg
        ntmp = xtmp if slot == 0 else xtmp2
        qeng = "sync" if dt % 2 == 0 else "scalar"
        rec.wait(qeng, xfree_ctr[slot], xfree_v[slot])
        rec.wait(qeng, cs, xfree_s[slot])
        xload[slot] = dma(qeng, dxs[slot], xs[slot][:], xT_in[dt * 128:(dt + 1) * 128, :])
        rec.wait(neng, dxs[slot], xload[slot])
        if slot == 1:
            rec.wait(neng, cv, stats_done)
        xfree_v[slot] = step(neng, lambda v, s=slot, tm=ntmp: v.tensor_tensor(
            tm[:], xs[s][:], m_b[:], ALU.subtract), nctr, dr=True)
        xfree_ctr[slot] = nctr
        if slot == 0:
            heater(xfree_v[slot], 5)
        rec.wait(neng, ct, lgdone[slot])
        rec.wait(neng, cs, xncopy[slot])
        x32d[slot] = step(neng, lambda v, s=slot, tm=ntmp: v.tensor_tensor(
            xn32[s][:], tm[:], r_b[:], ALU.mult), nctr, dr=True)
        rec.wait("scalar", nctr, x32d[slot])
        xncopy[slot] = step("scalar", lambda s_, s=slot, dt=dt: s_.activation(
            out=xn[dt][:], in_=xn32[s][:], func=AF.Copy), cs)
        rec.wait("tensor", nctr, max(x32d[slot], stats_done if slot == 0 else 0))
        rec.wait("tensor", cv, stats_done)
        rec.wait("tensor", dcq, consts_q)
        for tt in range(NTT):
            first = (dt == 0 and tt == 0)
            last_mm = (dt == NDT - 1 and tt == NTT - 1)
            lgdone[slot] = step("tensor", lambda t, s=slot, tt=tt, dt=dt, first=first, last_mm=last_mm: t.matmul(
                pd[0][:, tt * 8:(tt + 1) * 8], xn32[s][:, tt * 128:(tt + 1) * 128],
                gateWT32[:, dt * 8:(dt + 1) * 8],
                start=first, stop=last_mm),
                ct if tt == NTT - 1 else None) or lgdone[slot]

    lg_mm_done = lgdone[(NDT - 1) % 2]

    lshf = st_a[:, 0:128]
    p8f = st_a[:, 128:256]
    maskf = st_a[:, 256:384]
    gwpf = st_a[:, 384:512]
    mxs = st_b[:, 0:16]
    m2s = st_b[:, 16:32]
    d21s = st_b[:, 32:48]
    e1s = st_b[:, 48:64]
    dens = st_b[:, 64:80]
    rdens = st_b[:, 80:96]
    l3 = l_sb[:].rearrange("p (t e) -> p t e", e=8)
    rec.wait("vector", ct, lg_mm_done)
    rec.wait("vector", dcg, consts_g)
    lbias_ap = gateb_sb[:, None, :].broadcast_to([128, NTT, NE])
    lpsum_ap = pd[0][:, 0:128].rearrange("p (t e) -> p t e", e=8)
    step("vector", lambda v: v.tensor_tensor(l3, lpsum_ap, lbias_ap, ALU.add), dr=True)
    step("vector", lambda v: v.tensor_reduce(
        mxs[:].rearrange("p (t o) -> p t o", o=1), l3, mybir.AxisListType.X, ALU.max), dr=True)
    mx3 = mxs[:, :, None].broadcast_to([128, NTT, NE])
    step("vector", lambda v: v.tensor_tensor(
        maskf[:].rearrange("p (t e) -> p t e", e=8), l3, mx3, ALU.is_equal), dr=True)
    step("vector", lambda v: v.scalar_tensor_tensor(
        out=lshf[:].rearrange("p (t e) -> p t e", e=8),
        in0=maskf[:].rearrange("p (t e) -> p t e", e=8), scalar=-1e30, in1=l3,
        op0=ALU.mult, op1=ALU.add), dr=True)
    step("vector", lambda v: v.tensor_reduce(
        m2s[:].rearrange("p (t o) -> p t o", o=1),
        lshf[:].rearrange("p (t e) -> p t e", e=8), mybir.AxisListType.X, ALU.max), dr=True)
    d21_done = step("vector", lambda v: v.tensor_tensor(d21s[:], m2s[:], mxs[:], ALU.subtract), cv, dr=True)
    rec.wait("scalar", cv, d21_done)
    e1_done = step("scalar", lambda s: s.activation(out=e1s[:], in_=d21s[:], func=AF.Exp), cs)
    m23 = m2s[:, :, None].broadcast_to([128, NTT, NE])
    step("vector", lambda v: v.tensor_tensor(
        lshf[:].rearrange("p (t e) -> p t e", e=8), l3, mx3, ALU.subtract), dr=True)
    lsh_done = step("vector", lambda v: v.tensor_tensor(
        maskf[:].rearrange("p (t e) -> p t e", e=8), l3, m23, ALU.is_ge), cv, dr=True)
    rec.wait("scalar", cv, lsh_done)
    p8_done = step("scalar", lambda s: s.activation(out=p8f[:], in_=lshf[:], func=AF.Exp), cs)
    rec.wait("vector", cs, e1_done)
    step("vector", lambda v: v.tensor_scalar(
        out=dens[:], in0=e1s[:], scalar1=1.0, scalar2=None, op0=ALU.add), dr=True)
    step("vector", lambda v: v.reciprocal(rdens[:], dens[:]), dr=True)
    rec.wait("vector", cs, p8_done)
    step("vector", lambda v: v.tensor_tensor(gwpf[:], p8f[:], maskf[:], ALU.mult), dr=True)
    rd3 = rdens[:, :, None].broadcast_to([128, NTT, NE])
    gw_done = step("vector", lambda v: v.tensor_tensor(
        gw_sb[:].rearrange("p (t e) -> p t e", e=8),
        gwpf[:].rearrange("p (t e) -> p t e", e=8), rd3, ALU.mult), cv, dr=True)

    gwt_evac = 0
    for tt in range(NTT):
        bk = 6 + tt % 2
        rec.wait("tensor", cv, gw_done)
        rec.wait("tensor", cv, gwt_evac)
        tr_done = step("tensor", lambda t, bk=bk, tt=tt: t.matmul(
            pb[bk][0:8, 0:128], gw_sb[:, tt * 8:(tt + 1) * 8], id_f32[:],
            start=True, stop=True), ct)
        rec.wait("vector", ct, tr_done)
        gwt_evac = step("vector", lambda v, bk=bk, tt=tt: v.tensor_copy(
            gwT_sb[:, tt * 128:(tt + 1) * 128], pb[bk][0:8, 0:128]), cv)

    rec.wait("sync", cv, gwt_evac)
    gwT_hbm_done = dma("sync", dgww, gw_hbm[:], gwT_sb[:])
    gwTb_done = step("vector", lambda v: v.tensor_copy(gwTb_sb[:], gwT_sb[:]), cv, dr=True)

    if debug_upto == "p0":
        finish_debug(xn)
        return finalize()
    if debug_upto == "p1gw":
        finish_debug([gw_sb, gwT_sb, l_sb])
        return finalize()
    barrier()

    qksl_done = [0, 0, 0, 0]
    qk_mm_done = {}
    qk_evac_done = {}

    def emit_qksl_dma(ot):
        slot = ot % 4
        if ot >= 4:
            rec.wait("gpsimd", ct, qk_mm_done[ot - 4])
        qksl_done[slot] = dma("gpsimd", dqk[slot], qksl[slot][:],
                              wqkvT_in[:, ot * 128:(ot + 1) * 128].rearrange("(a p) o -> p a o", p=128))

    for _o in range(4):
        emit_qksl_dma(_o)
    for ot in range(16):
        slot = ot % 4
        bs = (ot % 2) * 4
        rec.wait("tensor", dqk[slot], qksl_done[slot])
        if ot >= 2:
            rec.wait("tensor", cv, qk_evac_done[ot - 2])
        for dt in range(NDT):
            for c in range(NTC):
                last = (dt == NDT - 1 and c == NTC - 1)
                qk_mm_done[ot] = step("tensor", lambda t, s=slot, bs=bs, dt=dt, c=c: t.matmul(
                    pb[bs + c][:], qksl[s][:, dt * 128:(dt + 1) * 128],
                    xn[dt][:, c * 512:(c + 1) * 512],
                    start=(dt == 0), stop=(dt == NDT - 1)), ct if last else None) or qk_mm_done.get(ot, 0)
        if ot + 4 < 16:
            emit_qksl_dma(ot + 4)
        rec.wait("vector", ct, qk_mm_done[ot])
        rec.wait("vector", dcq, consts_q)
        dest = qT[ot] if ot < 8 else kT[ot - 8]
        for c in range(NTC):
            last = (c == NTC - 1)
            qk_evac_done[ot] = step("vector", lambda v, d=dest, bs=bs, c=c, ot=ot: v.tensor_scalar(
                out=d[:, c * 512:(c + 1) * 512], in0=pb[bs + c][:],
                scalar1=bqkv_cols[:, ot:ot + 1], scalar2=None, op0=ALU.add),
                cv if last else None) or qk_evac_done.get(ot, 0)

    vsl_done = [0, 0]
    for c in range(2):
        vsl_done[c] = dma("gpsimd", dvs[c], vsl[c][:],
                          wqkvT_in[:, 2 * D + c * 512: 2 * D + (c + 1) * 512]
                          .rearrange("(a p) o -> p a o", p=128))
    v_mm_done = {}
    v_evac_done = {}
    for tt in range(NTT):
        bs = (tt % 2) * 2
        rec.wait("tensor", dvs[0], vsl_done[0])
        rec.wait("tensor", dvs[1], vsl_done[1])
        rec.wait("tensor", cv, qk_evac_done[15])
        if tt >= 2:
            rec.wait("tensor", cv, v_evac_done[tt - 2])
        for dt in range(NDT):
            for c in range(2):
                last = (dt == NDT - 1 and c == 1)
                v_mm_done[tt] = step("tensor", lambda t, bs=bs, dt=dt, c=c, tt=tt: t.matmul(
                    pb[bs + c][:], xn[dt][:, tt * 128:(tt + 1) * 128],
                    vsl[c][:, dt * 512:(dt + 1) * 512],
                    start=(dt == 0), stop=(dt == NDT - 1)), ct if last else None) or v_mm_done.get(tt, 0)
        rec.wait("vector", ct, v_mm_done[tt])
        rec.wait("vector", dcg, consts_g)
        step("vector", lambda v, tt=tt: v.memset(Vaug[tt][:, 64:NH * 65:65], 1.0))
        for c in range(2):
            last = (c == 1)
            out_ap = Vaug[tt][:, c * 520:(c + 1) * 520].rearrange("p (h x) -> p h x", x=65)[:, :, 0:64]
            in1_ap = bvb[:, c * 512:(c + 1) * 512].rearrange("p (h x) -> p h x", x=64)
            psum_ap = pb[bs + c][:].rearrange("p (h x) -> p h x", x=64)
            v_evac_done[tt] = step("vector", lambda v, o_=out_ap, p_=psum_ap, b_=in1_ap: v.tensor_tensor(
                o_, p_, b_, ALU.add), cv if last else None) or v_evac_done.get(tt, 0)

    if debug_upto == "p1q":
        finish_debug(qT)
        return finalize()
    if debug_upto == "p1k":
        finish_debug(kT)
        return finalize()
    if debug_upto == "p1v":
        finish_debug(Vaug[:8])
        return finalize()
    if debug_upto == "p1v2":
        finish_debug(Vaug[8:])
        return finalize()
    barrier()

    iters = []
    for g in range(NG):
        klist = [kt for kt in range(2 * g - 1, 2 * g + 3) if 0 <= kt < NTT]
        for h in range(NH):
            iters.append((g, h, klist))
    NIT = len(iters)

    dots_free = {}
    av_free = {}
    exp_free = {}
    dots_done_l = {}
    exp_done_l = {}

    def emit_dots_pair(it0, it1):
        mms = []
        for it in (it0, it1):
            g, h, klist = iters[it]
            par = it % 3
            if it >= 3:
                rec.wait("tensor", cs, dots_free[par])
            pt = h // 2
            po = (h % 2) * 64
            for j, kt in enumerate(klist):
                mms.append((it, par, pt, po, j, kt, g, len(klist)))
        seq = []
        a = [m for m in mms if m[0] == it0]
        b = [m for m in mms if m[0] == it1]
        for i in range(max(len(a), len(b))):
            if i < len(a):
                seq.append(a[i])
            if i < len(b):
                seq.append(b[i])
        for (it, par, pt, po, j, kt, g, nk) in seq:
            last = (j == nk - 1)
            dd = step("tensor", lambda t, par=par, pt=pt, po=po, j=j, kt=kt, g=g: t.matmul(
                pd[par][:, j * 256:(j + 1) * 256],
                kT[pt][po:po + 64, kt * 128:(kt + 1) * 128],
                qT[pt][po:po + 64, g * 256:(g + 1) * 256],
                start=True, stop=True), ct if last else None)
            if last:
                dots_done_l[it] = dd

    def emit_exp(it):
        g, h, klist = iters[it]
        par = it % 3
        nk = len(klist)
        rec.wait("scalar", ct, dots_done_l[it])
        if it >= 3:
            rec.wait("scalar", ct, exp_free[par])
        exp_done_l[it] = step("scalar", lambda s, e=par, par=par, nk=nk: s.activation(
            out=expb[e][:, 0:nk * 256], in_=pd[par][:, 0:nk * 256],
            func=AF.Exp, bias=c_shift[:], scale=1.0 / 8.0), cs)
        dots_free[par] = exp_done_l[it]

    def emit_av(it):
        g, h, klist = iters[it]
        par = it % 3
        abk = 6 + it % 2
        nk = len(klist)
        rec.wait("tensor", cs, exp_done_l[it])
        if it >= 2:
            rec.wait("tensor", cv, av_free[it % 2])
        av_done = 0
        for qi in range(2):
            for j, kt in enumerate(klist):
                last = (qi == 1 and j == nk - 1)
                av_done = step("tensor", lambda t, e=par, j=j, qi=qi, kt=kt, abk=abk, h=h, nk=nk: t.matmul(
                    pb[abk][:, qi * 65:qi * 65 + 65],
                    expb[e][:, j * 256 + qi * 128:j * 256 + qi * 128 + 128],
                    Vaug[kt][:, h * 65:(h + 1) * 65],
                    start=(j == 0), stop=(j == nk - 1)), ct if last else None) or av_done
        exp_free[par] = av_done
        rec.wait("vector", ct, av_done)
        ev = 0
        for qi in range(2):
            rcc = (it % 2) * 2 + qi
            step("vector", lambda v, abk=abk, qi=qi, rcc=rcc: v.reciprocal(
                rc_sb[:, rcc:rcc + 1], pb[abk][:, qi * 65 + 64:qi * 65 + 65]))
            ev = step("vector", lambda v, abk=abk, qi=qi, g=g, h=h, rcc=rcc: v.tensor_scalar(
                out=AO[g * 2 + qi][:, h * 64:(h + 1) * 64],
                in0=pb[abk][:, qi * 65:qi * 65 + 64],
                scalar1=rc_sb[:, rcc:rcc + 1], scalar2=None, op0=ALU.mult),
                cv if qi == 1 else None, dr=True) or ev
        av_free[it % 2] = ev

    emit_dots_pair(0, 1)
    for s in range(0, NIT, 2):
        emit_exp(s)
        emit_exp(s + 1)
        if s + 2 < NIT:
            emit_dots_pair(s + 2, s + 3)
        emit_av(s)
        emit_av(s + 1)

    if debug_upto == "p2a":
        finish_debug(AO[:8])
        return finalize()
    if debug_upto == "p2b":
        finish_debug(AO[8:])
        return finalize()
    barrier()

    wp_done = dma("gpsimd", dwp, wp_sb[:],
                  wprojT_in[:, :].rearrange("(a p) o -> p a o", p=128))

    tr_evac = {}
    for tt in range(NTT):
        bk = tt % 2
        if tt >= 2:
            rec.wait("tensor", cv, tr_evac[tt - 2])
        tdone = 0
        for c8 in range(8):
            last = (c8 == 7)
            tdone = step("tensor", lambda t, tt=tt, c8=c8, bk=bk: t.matmul(
                pd[bk][:, c8 * 128:(c8 + 1) * 128],
                AO[tt][:, c8 * 128:(c8 + 1) * 128], id_bf16[:],
                start=True, stop=True),
                ct if last else None) or tdone
        rec.wait("vector", ct, tdone)
        out_ap = AOT_sb[:, :].rearrange("p (c x) -> p c x", x=T)[:, :, tt * 128:(tt + 1) * 128]
        in_ap = pd[bk][:].rearrange("p (c x) -> p c x", x=128)
        tr_evac[tt] = step("vector", lambda v, o_=out_ap, i_=in_ap: v.tensor_copy(o_, i_), cv)

    proj_mm = {}
    proj_evac = {}
    for ot in range(NDT):
        bs = (ot % 2) * 4
        rec.wait("tensor", dwp, wp_done)
        rec.wait("tensor", cv, tr_evac[NTT - 1])
        if ot >= 2:
            rec.wait("tensor", cv, proj_evac[ot - 2])
        for dt in range(NDT):
            for c in range(NTC):
                last = (dt == NDT - 1 and c == NTC - 1)
                proj_mm[ot] = step("tensor", lambda t, ot=ot, dt=dt, c=c, bs=bs: t.matmul(
                    pb[bs + c][:], wp_sb[:, dt * D + ot * 128:dt * D + ot * 128 + 128],
                    AOT_sb[:, dt * T + c * 512:dt * T + (c + 1) * 512],
                    start=(dt == 0), stop=(dt == NDT - 1)), ct if last else None) or proj_mm.get(ot, 0)
        rec.wait("vector", ct, proj_mm[ot])
        for c in range(NTC):
            last = (c == NTC - 1)
            proj_evac[ot] = step("vector", lambda v, ot=ot, c=c, bs=bs: v.tensor_scalar(
                out=SUM[ot][:, c * 512:(c + 1) * 512], in0=pb[bs + c][:],
                scalar1=bproj_cols[:, ot:ot + 1], scalar2=None, op0=ALU.add),
                cv if last else None) or proj_evac.get(ot, 0)

    b2_evac = 0
    for dt in range(NDT):
        for c in range(NTC):
            bk = 6 + (dt * NTC + c) % 2
            rec.wait("tensor", cv, max(gwTb_done, b2_evac, proj_evac[NDT - 1]))
            rec.wait("tensor", dcg, consts_g)
            bdone = step("tensor", lambda t, dt=dt, c=c, bk=bk: t.matmul(
                pb[bk][:], b2sb[:, dt * 128:(dt + 1) * 128],
                gwTb_sb[:, c * 512:(c + 1) * 512], start=True, stop=True), ct)
            rec.wait("vector", ct, bdone)
            rec.wait("vector", cv, proj_evac.get(dt, 0))
            b2_evac = step("vector", lambda v, dt=dt, c=c, bk=bk: v.tensor_tensor(
                SUM[dt][:, c * 512:(c + 1) * 512], SUM[dt][:, c * 512:(c + 1) * 512],
                pb[bk][:], ALU.add), cv)

    if debug_upto == "p3":
        finish_debug(SUM)
        return finalize()

    w2_loaded = 0
    gwb_loaded = 0
    sl_loaded = {}
    h_mm_done = {}
    act_evac = {}
    w2_evac_d = {}
    b_last_mm = 0

    def emit_slice_dma(e, ht):
        slot = ht % 4
        if (e, ht - 4) in h_mm_done:
            rec.wait("gpsimd", ct, h_mm_done[(e, ht - 4)])
        elif e > 0 and ht < 4:
            rec.wait("gpsimd", ct, h_mm_done[(e - 1, NHT - 4 + ht)])
        a = dma("gpsimd", dw0[slot], w0sl[slot][:],
                w0T_in[e][:, ht * 128:(ht + 1) * 128].rearrange("(a p) o -> p a o", p=128))
        b = dma("gpsimd", dw1[slot], w1sl[slot][:],
                w1T_in[e][:, ht * 128:(ht + 1) * 128].rearrange("(a p) o -> p a o", p=128))
        sl_loaded[(e, ht)] = (slot, a, b)


    rec.wait("gpsimd", ct, proj_mm[NDT - 1])
    for _h in range(4):
        emit_slice_dma(0, _h)
    rec.wait("gpsimd", dgww, gwT_hbm_done)
    w2_loaded = dma("gpsimd", dw2, W2T_sb[:],
                    w2T_in[0][:, :].rearrange("(a p) d -> p a d", p=128))
    gwb_loaded = dma("gpsimd", dgwb, gwb_sb[:], gw_hbm[0][None, :].partition_broadcast(128))
    barrier()


    for e in range(NE):
        for ht in range(NHT):
            slot = ht % 4
            bcol = e * 16 + ht
            _sl, _a, _b = sl_loaded[(e, ht)]
            rec.wait("tensor", dw0[_sl], _a)
            rec.wait("tensor", dw1[_sl], _b)
            for tc in range(NTC):
                i = ht * NTC + tc
                par = i % 3
                if i >= 3:
                    rec.wait("tensor", cv, act_evac[(e, i - 3)])
                for dt in range(NDT):
                    step("tensor", lambda t, s=slot, dt=dt, tc=tc, par=par: t.matmul(
                        pb[par * 2][:], w0sl[s][:, dt * 128:(dt + 1) * 128],
                        xn[dt][:, tc * 512:(tc + 1) * 512],
                        start=(dt == 0), stop=(dt == NDT - 1)))
                for dt in range(NDT):
                    last = (dt == NDT - 1)
                    h_mm_done[(e, ht)] = step("tensor", lambda t, s=slot, dt=dt, tc=tc, par=par: t.matmul(
                        pb[par * 2 + 1][:], w1sl[s][:, dt * 128:(dt + 1) * 128],
                        xn[dt][:, tc * 512:(tc + 1) * 512],
                        start=(dt == 0), stop=(dt == NDT - 1)), ct if last else None) \
                        or h_mm_done.get((e, ht), 0)
                hm = h_mm_done[(e, ht)]
                rec.wait("scalar", ct, hm)
                if i >= 3:
                    rec.wait("scalar", cv, act_evac[(e, i - 3)])
                sil_done = step("scalar", lambda s, par=par, bcol=bcol: s.activation(
                    out=silb[par][:], in_=pb[par * 2 + 1][:], func=AF.Silu,
                    bias=b1cols[:, bcol:bcol + 1], scale=1.0), cs)
                rec.wait("vector", cs, sil_done)
                if e > 0:
                    rec.wait("vector", ct, b_last_mm)
                act_evac[(e, i)] = step("vector", lambda v, ht=ht, tc=tc, par=par, bcol=bcol: v.scalar_tensor_tensor(
                    out=act[ht][:, tc * 512:(tc + 1) * 512], in0=pb[par * 2][:],
                    scalar=b0cols[:, bcol:bcol + 1], in1=silb[par][:],
                    op0=ALU.add, op1=ALU.mult), cv)
            if ht + 4 < NHT:
                emit_slice_dma(e, ht + 4)
            elif e + 1 < NE:
                emit_slice_dma(e + 1, ht + 4 - NHT)

        rec.wait("tensor", dw2, w2_loaded)
        rec.wait("vector", dgwb, gwb_loaded)
        last_act = act_evac[(e, NHT * NTC - 1)]
        rec.wait("tensor", cv, last_act)
        for tc in range(NTC):
            for dt in range(NDT):
                j = tc * NDT + dt
                bk = 6 + j % 2
                if j >= 2:
                    rec.wait("tensor", cv, w2_evac_d[(e, j - 2)])
                for ht in range(NHT):
                    last = (ht == NHT - 1)
                    b_last_mm = step("tensor", lambda t, ht=ht, dt=dt, tc=tc, bk=bk: t.matmul(
                        pb[bk][:], W2T_sb[:, ht * D + dt * 128:ht * D + dt * 128 + 128],
                        act[ht][:, tc * 512:(tc + 1) * 512],
                        start=(ht == 0), stop=(ht == NHT - 1)), ct if last else None) or b_last_mm
                rec.wait("vector", ct, b_last_mm)
                step("vector", lambda v, tc=tc, bk=bk, j=j: v.tensor_tensor(
                    tmpb[0][:], pb[bk][:], gwb_sb[:, tc * 512:(tc + 1) * 512], ALU.mult), dr=True)
                w2_evac_d[(e, j)] = step("vector", lambda v, dt=dt, tc=tc, j=j: v.tensor_tensor(
                    SUM[dt][:, tc * 512:(tc + 1) * 512], SUM[dt][:, tc * 512:(tc + 1) * 512],
                    tmpb[0][:], ALU.add), cv, dr=True)
        if e + 1 < NE:
            rec.wait("gpsimd", ct, b_last_mm)
            w2_loaded = dma("gpsimd", dw2, W2T_sb[:],
                            w2T_in[e + 1][:, :].rearrange("(a p) d -> p a d", p=128))
            rec.wait("gpsimd", cv, w2_evac_d[(e, NTC * NDT - 1)])
            gwb_loaded = dma("gpsimd", dgwb, gwb_sb[:],
                             gw_hbm[e + 1][None, :].partition_broadcast(128))

    if debug_upto == "p4":
        finish_debug(SUM)
        return finalize()

    xload = [0, 0]
    add_done = [0, 0]
    store_done = [0, 0]
    a7_done = h_mm_done[(NE - 1, NHT - 1)]
    for dt in range(NDT):
        slot = dt % 2
        qeng = "sync" if dt % 2 == 0 else "scalar"
        rec.wait(qeng, cv if dt % 2 == 0 else cg, add_done[slot])
        rec.wait(qeng, ct, a7_done)
        xload[slot] = dma(qeng, dxs[slot], xs5[slot][:], xT_in[dt * 128:(dt + 1) * 128, :])
        aeng = "vector" if dt % 2 == 0 else "gpsimd"
        actr = cv if dt % 2 == 0 else cg
        rec.wait(aeng, dxs[slot], xload[slot])
        if dt >= 2:
            rec.wait(aeng, dstr[slot], store_done[slot])
        rec.wait(aeng, ct, b_last_mm)
        rec.wait(aeng, cv, w2_evac_d[(NE - 1, 3 * NDT + dt)])
        add_done[slot] = step(aeng, lambda v, s=slot, dt=dt: v.tensor_tensor(
            ob[s][:], SUM[dt][:], xs5[s][:], ALU.add), actr)
        seng = "sync" if dt % 2 == 0 else "scalar"
        rec.wait(seng, actr, add_done[slot])
        store_done[slot] = dma(seng, dstr[slot], out_hbm[dt * 128:(dt + 1) * 128, :], ob[slot][:])

    rec.emit("sync", lambda E, v=dstr[0].n: E.wait_ge(dstr[0].sem, v))
    rec.emit("sync", lambda E, v=dstr[1].n: E.wait_ge(dstr[1].sem, v))


    return finalize()


def kernel(L0_attn_x, L0_attn_w_qkv, L0_attn_b_qkv, L0_attn_w_proj, L0_attn_b_proj,
           L0_ffn_gate_w, L0_ffn_gate_b, L0_ffn_w0, L0_ffn_b0, L0_ffn_w1, L0_ffn_b1,
           L0_ffn_w2, L0_ffn_b2):
    x = np.asarray(L0_attn_x, dtype=np.float32)

    shared = {
        "wqkvT": np.ascontiguousarray(np.asarray(L0_attn_w_qkv, np.float32).T),
        "wprojT": np.ascontiguousarray(np.asarray(L0_attn_w_proj, np.float32).T),
        "gateWT": np.ascontiguousarray(np.asarray(L0_ffn_gate_w, np.float32).T),
        "w0T": np.ascontiguousarray(np.asarray(L0_ffn_w0, np.float32).transpose(0, 2, 1)),
        "w1T": np.ascontiguousarray(np.asarray(L0_ffn_w1, np.float32).transpose(0, 2, 1)),
        "w2T": np.ascontiguousarray(np.asarray(L0_ffn_w2, np.float32).transpose(0, 2, 1)),
        "bqkv_qk": np.ascontiguousarray(np.asarray(L0_attn_b_qkv, np.float32)[:2 * D].reshape(16, 128).T),
        "bqkv_v": np.ascontiguousarray(np.asarray(L0_attn_b_qkv, np.float32)[2 * D:]),
        "bproj": np.ascontiguousarray(np.asarray(L0_attn_b_proj, np.float32).reshape(8, 128).T),
        "b0c": np.ascontiguousarray(np.asarray(L0_ffn_b0, np.float32).reshape(8, 16, 128).transpose(2, 0, 1).reshape(128, 128)),
        "b1c": np.ascontiguousarray(np.asarray(L0_ffn_b1, np.float32).reshape(8, 16, 128).transpose(2, 0, 1).reshape(128, 128)),
        "b2": np.ascontiguousarray(np.asarray(L0_ffn_b2, np.float32)),
        "gateb": np.ascontiguousarray(np.asarray(L0_ffn_gate_b, np.float32)),
        "ident": np.eye(128, dtype=np.float32),
    }

    in_maps = []
    for c in range(8):
        m = dict(shared)
        m["xT"] = np.ascontiguousarray(x[c].T)
        in_maps.append(m)

    nc = build_program()
    res = run_bass_kernel_spmd(nc, in_maps, core_ids=list(range(8)))
    out = np.stack([np.asarray(res.results[c]["outT"]).T for c in range(8)], axis=0)
    return np.ascontiguousarray(out.astype(np.float32))
